# version 01 (final)
# baseline (speedup 1.0000x reference)
import numpy as np
from contextlib import ExitStack
import concourse.bass as bass
import concourse.mybir as mybir

F32 = mybir.dt.float32
BF16 = mybir.dt.bfloat16
I32 = mybir.dt.int32
AF = mybir.ActivationFunctionType
ALU = mybir.AluOpType

D_MODEL = 1024
D_INNER = 2048
DT_RANK = 64
D_STATE = 16
SEQ = 16384
NB = 2
T_OWN = 4096
E_SL = 256
DFF = 4096
EPS = 1e-5


class Ctx:
    def __init__(self, nc):
        self.nc = nc
        self.es = ExitStack()

    def sb(self, name, shape, dtype):
        return self.es.enter_context(self.nc.sbuf_tensor(name, shape, dtype))

    def ps(self, name, shape, dtype=F32):
        return self.es.enter_context(self.nc.psum_tensor(name, shape, dtype))

    def sem(self, name):
        return self.es.enter_context(self.nc.semaphore(name))


ENGS = ("sync", "tensor", "vector", "scalar", "gpsimd")


class Sched:

    def __init__(self, ctx):
        self.sem = ctx.sem("chain")
        self.plan = {e: [] for e in ENGS}
        self.val = 0
        self.nsteps = 0
        self.limit = None

    def step(self, eng, emit, weight=1, wait=None):
        if self.limit is not None and self.nsteps >= self.limit:
            return self.val
        self.nsteps += 1
        w = self.val if wait is None else wait
        self.plan[eng].append((w, emit, weight))
        self.val += weight
        return self.val

    def dma(self, eng, out, in_, wait=None, slow=False):
        def emit(e):
            return e.dma_start(out=out, in_=in_, allow_slow_non_contiguous=slow)
        return self.step(eng, emit, weight=16, wait=wait)

    def emit_all(self, block):
        sem = self.sem

        def mk(eng_name):
            def body(eng):
                last_wait = -1
                for (w, fn, wt) in self.plan[eng_name]:
                    if w > 0 and w > last_wait:
                        eng.wait_ge(sem, w)
                        last_wait = w
                    inst = fn(eng)
                    assert inst is not None, f"step on {eng_name} returned None"
                    inst.then_inc(sem, wt)
            return body

        for name in ENGS:
            getattr(block, name)(mk(name))


def final_wait(sk, ctx):
    v = sk.val

    def emit(e):
        e.wait_ge(sk.sem, v)
        return e.engine_nop() if hasattr(e, "engine_nop") else e.nop()
    sk.step("sync", lambda e: (e.wait_ge(sk.sem, v), e.nop())[1], weight=1)



def emit_ident(sk, ctx, ident, n=128):
    def emit(g):
        g.memset(ident[:], 0.0)
        return g.affine_select(out=ident[:], in_=ident[:], compare_op=ALU.not_equal,
                               fill=1.0, base=0, pattern=[[-1, n]], channel_multiplier=1)
    sk.step("gpsimd", emit)


def emit_ones(sk, ctx, ones_ap, val=1.0):
    sk.step("gpsimd", lambda g: g.memset(ones_ap, val))


def emit_mod_rows(sk, ctx, *, c_sel, swmod_hbm, slots, row_out, csilu, csiluT,
                  ident, pst, psrow0, psrow1, swk, nk=8):
    sk.step("scalar", lambda s: s.activation(csilu[:], c_sel[:], AF.Silu))
    def tr(t):
        for i in range(nk):
            inst = t.transpose(pst[:, i:i + 1], csilu[:, i * 128:(i + 1) * 128],
                               ident[0:1, 0:1])
        return inst
    sk.step("tensor", tr)
    sk.step("vector", lambda v: v.tensor_copy(csiluT[:], pst[:, 0:nk]))
    for s in slots:
        for k in range(nk):
            sk.dma("sync", swk[:], swmod_hbm[k * 128:(k + 1) * 128,
                                             s * 1024:(s + 1) * 1024])
            def mk_mm(k=k):
                def mm(t):
                    t.matmul(psrow0, csiluT[:, k:k + 1], swk[:, 0:512],
                             start=(k == 0), stop=(k == nk - 1))
                    return t.matmul(psrow1, csiluT[:, k:k + 1], swk[:, 512:1024],
                                    start=(k == 0), stop=(k == nk - 1))
                return mm
            sk.step("tensor", mk_mm(k))
        def evac(v, s=s):
            v.tensor_copy(row_out[s][:, 0:512], psrow0)
            return v.tensor_copy(row_out[s][:, 512:1024], psrow1)
        sk.step("vector", evac)


def emit_replicate_row(sk, ctx, *, row, ones128, psrep, out, bias=None, nchunk=2,
                       width=512):
    def mm(t):
        for j in range(nchunk):
            inst = t.matmul(psrep[:, j * width:(j + 1) * width][:, 0:width] if False else psrep[:],
                            ones128[:], row[:, j * width:(j + 1) * width],
                            start=True, stop=True)
        return inst
    for j in range(nchunk):
        def mk(j=j):
            return lambda t: t.matmul(psrep[:], ones128[:],
                                      row[:, j * width:(j + 1) * width],
                                      start=True, stop=True)
        sk.step("tensor", mk(j))
        if bias is None:
            def mkv(j=j):
                return lambda v: v.tensor_copy(out[:, j * width:(j + 1) * width], psrep[:])
            sk.step("vector", mkv(j))
        else:
            def mks(j=j):
                return lambda s: s.activation(out[:, j * width:(j + 1) * width],
                                              psrep[:], AF.Copy, bias=bias)
            sk.step("scalar", mks(j))



def build_L1():
    nc = bass.Bass("TRN2", target_bir_lowering=False)
    x = nc.dram_tensor("x", [T_OWN, D_MODEL], F32, kind="ExternalInput").ap()
    c_sel = nc.dram_tensor("c_sel", [1, D_MODEL], F32, kind="ExternalInput").ap()
    wmod = nc.dram_tensor("wmod", [D_MODEL, 6 * D_MODEL], F32, kind="ExternalInput").ap()
    h_out = nc.dram_tensor("h", [T_OWN, D_MODEL], BF16, kind="ExternalOutput").ap()
    r2_out = nc.dram_tensor("r2o", [128, 32], F32, kind="ExternalOutput").ap()

    ctx = Ctx(nc)
    sk = Sched(ctx)
    NT = T_OWN // 128

    sc = ctx.sb("sc", [1, D_MODEL], F32)
    csilu = ctx.sb("csilu", [1, D_MODEL], F32)
    csiluT = ctx.sb("csiluT", [128, 8], F32)
    ident = ctx.sb("ident", [128, 128], F32)
    ones128 = ctx.sb("ones128", [1, 128], F32)
    row_sa = ctx.sb("row_sa", [1, D_MODEL], F32)
    row_ca = ctx.sb("row_ca", [1, D_MODEL], F32)
    carep1 = ctx.sb("carep1", [128, D_MODEL], BF16)
    sarep = ctx.sb("sarep", [128, D_MODEL], BF16)
    xt = [ctx.sb(f"xt{i}", [128, D_MODEL], F32) for i in range(2)]
    sq = ctx.sb("sq", [128, D_MODEL], F32)
    t1 = ctx.sb("t1", [128, D_MODEL], BF16)
    t2 = ctx.sb("t2", [128, D_MODEL], BF16)
    ht = ctx.sb("ht", [128, D_MODEL], BF16)
    s1c = ctx.sb("s1c", [128, NT], F32)
    s2c = ctx.sb("s2c", [128, NT], F32)
    hsqc = ctx.sb("hsqc", [128, NT], F32)
    negm = ctx.sb("negm", [128, NT], F32)
    r1 = ctx.sb("r1", [128, NT], F32)
    r2 = ctx.sb("r2", [128, NT], F32)
    tmpc = ctx.sb("tmpc", [128, NT], F32)
    tmpc2 = ctx.sb("tmpc2", [128, NT], F32)

    pst = ctx.ps("pst", [128, 8], F32)
    psrep = ctx.ps("psrep", [128, 512], F32)
    psr2 = ctx.ps("psr2", [128, 512], F32)

    sk.dma("sync", sc[:], c_sel)
    emit_ident(sk, ctx, ident)
    emit_ones(sk, ctx, ones128[:], 1.0)
    swk1 = ctx.sb("swk1", [128, 1024], F32)
    emit_mod_rows(sk, ctx, c_sel=sc, swmod_hbm=wmod, slots=[0, 1],
                  row_out={0: row_sa, 1: row_ca}, csilu=csilu, csiluT=csiluT,
                  ident=ident, pst=pst, psrow0=psrep[0:1, 0:512],
                  psrow1=pst[0:1, 0:8][0:1, 0:8] if False else psr2[0:1, 0:512],
                  swk=swk1)
    emit_replicate_row(sk, ctx, row=row_ca, ones128=ones128, psrep=psrep,
                       out=carep1, bias=1.0)
    emit_replicate_row(sk, ctx, row=row_sa, ones128=ones128, psrep=psrep,
                       out=sarep)

    for j in range(NT):
        xb = xt[j % 2]
        sk.dma("sync", xb[:], x[j * 128:(j + 1) * 128, :])
        def mk1(j=j, xb=xb):
            return lambda v: v.reduce_sum(s1c[:, j:j + 1], xb[:], axis=mybir.AxisListType.X)
        sk.step("vector", mk1())
        def mk2(xb=xb):
            return lambda s: s.activation(sq[:], xb[:], AF.Square)
        sk.step("scalar", mk2())
        def mk3(j=j):
            return lambda v: v.reduce_sum(s2c[:, j:j + 1], sq[:], axis=mybir.AxisListType.X)
        sk.step("vector", mk3())
    sk.step("vector", lambda v: v.tensor_scalar(negm[:], s1c[:], -1.0 / D_MODEL, None, ALU.mult))
    sk.step("vector", lambda v: v.tensor_tensor(tmpc[:], negm[:], negm[:], ALU.mult))
    sk.step("vector", lambda v: v.tensor_scalar(tmpc2[:], s2c[:], 1.0 / D_MODEL, None, ALU.mult))
    sk.step("vector", lambda v: v.tensor_tensor(tmpc[:], tmpc2[:], tmpc[:], ALU.subtract))
    sk.step("vector", lambda v: v.tensor_scalar(tmpc[:], tmpc[:], EPS, None, ALU.add))
    sk.step("vector", lambda v: v.reciprocal(tmpc[:], tmpc[:]))
    sk.step("scalar", lambda s: s.activation(r1[:], tmpc[:], AF.Sqrt))

    for j in range(NT):
        xb = xt[j % 2]
        sk.dma("sync", xb[:], x[j * 128:(j + 1) * 128, :])
        def mkh(j=j, xb=xb):
            def f(v):
                v.tensor_scalar(t1[:], xb[:], negm[:, j:j + 1], r1[:, j:j + 1],
                                ALU.add, ALU.mult)
                v.tensor_tensor(t2[:], t1[:], carep1[:], ALU.mult)
                return v.tensor_tensor(ht[:], t2[:], sarep[:], ALU.add)
            return f
        sk.step("vector", mkh())
        def mksq():
            return lambda s: s.activation(sq[:], ht[:], AF.Square)
        sk.step("scalar", mksq())
        def mkr(j=j):
            return lambda v: v.reduce_sum(hsqc[:, j:j + 1], sq[:], axis=mybir.AxisListType.X)
        sk.step("vector", mkr())
        sk.dma("sync", h_out[j * 128:(j + 1) * 128, :], ht[:])
    sk.step("vector", lambda v: v.tensor_scalar(tmpc[:], hsqc[:], 1.0 / D_MODEL, EPS,
                                                ALU.mult, ALU.add))
    sk.step("vector", lambda v: v.reciprocal(tmpc[:], tmpc[:]))
    sk.step("scalar", lambda s: s.activation(r2[:], tmpc[:], AF.Sqrt))
    sk.dma("sync", r2_out, r2[:])

    v = sk.val
    sk.plan["sync"].append((v, lambda e: e.nop(), 1))

    blk = ctx.es.enter_context(nc.Block())
    sk.emit_all(blk)
    ctx.es.close()
    return nc



def build_L2(nchunk=32, tc=1024, limit=None, debug=False):
    nc = bass.Bass("TRN2", target_bir_lowering=False)
    NS = tc // 512
    NTG = tc // 512
    BCH = SEQ // tc
    h_full = nc.dram_tensor("h_full", [NB * SEQ, D_MODEL], BF16, kind="ExternalInput").ap()
    r2_full = nc.dram_tensor("r2_full", [1, NB * SEQ], F32, kind="ExternalInput").ap()
    wi_xc = nc.dram_tensor("wi_xc", [E_SL, D_MODEL], F32, kind="ExternalInput").ap()
    wi_res = nc.dram_tensor("wi_res", [E_SL, D_MODEL], F32, kind="ExternalInput").ap()
    norm_w = nc.dram_tensor("norm_w", [1, D_MODEL], F32, kind="ExternalInput").ap()
    convw = nc.dram_tensor("convw", [E_SL, 4], F32, kind="ExternalInput").ap()
    convb = nc.dram_tensor("convb", [E_SL], F32, kind="ExternalInput").ap()
    wxp = nc.dram_tensor("wxp", [96, E_SL], F32, kind="ExternalInput").ap()
    wdt = nc.dram_tensor("wdt", [E_SL, DT_RANK], F32, kind="ExternalInput").ap()
    bdt = nc.dram_tensor("bdt", [E_SL], F32, kind="ExternalInput").ap()
    alog = nc.dram_tensor("alog", [E_SL, D_STATE], F32, kind="ExternalInput").ap()
    ddvec = nc.dram_tensor("ddvec", [E_SL], F32, kind="ExternalInput").ap()
    yg_out = nc.dram_tensor("yg", [E_SL, NB * SEQ], BF16, kind="ExternalOutput").ap()
    cc_in = nc.dram_tensor("cc_in", [96, tc], F32, kind="Internal").ap()
    cc_out = nc.dram_tensor("cc_out", [96, tc], F32, kind="Internal").ap()
    if debug:
        dbg_xs = nc.dram_tensor("dbg_xs", [128, 2, tc], BF16, kind="ExternalOutput").ap()
        dbg_delta = nc.dram_tensor("dbg_delta", [128, 2, tc], F32, kind="ExternalOutput").ap()
        dbg_dbc = nc.dram_tensor("dbg_dbc", [96, tc], BF16, kind="ExternalOutput").ap()
        dbg_res = nc.dram_tensor("dbg_res", [128, 2, tc], BF16, kind="ExternalOutput").ap()
        dbg_xc = nc.dram_tensor("dbg_xc", [128, 2, tc + 3], BF16, kind="ExternalOutput").ap()
        dbg_da = nc.dram_tensor("dbg_da", [128, tc], F32, kind="ExternalOutput").ap()
        dbg_dbu = nc.dram_tensor("dbg_dbu", [128, tc], BF16, kind="ExternalOutput").ap()
        dbg_h = nc.dram_tensor("dbg_h", [128, tc], BF16, kind="ExternalOutput").ap()
        dbg_z = nc.dram_tensor("dbg_z", [128, tc], BF16, kind="ExternalOutput").ap()
        dbg_y = nc.dram_tensor("dbg_y", [128, 2, tc], BF16, kind="ExternalOutput").ap()

    ctx = Ctx(nc)
    sk = Sched(ctx)
    sk.limit = limit

    pb = [ctx.ps(f"pb{i}", [128, 512], F32) for i in range(4)]
    psB = ctx.ps("psB", [128, tc], F32)
    psC = ctx.ps("psC", [128, tc], F32)

    ident = ctx.sb("ident", [128, 128], F32)
    identB = ctx.sb("identB", [128, 128], BF16)
    onesr = ctx.sb("onesr", [1, 512], F32)
    onesrB = ctx.sb("onesrB", [1, 128], BF16)
    selBC = ctx.sb("selBC", [96, 32, 128], BF16)
    wiT = ctx.sb("wiT", [128, 8, 512], BF16)
    nwT = ctx.sb("nwT", [128, 8], F32)
    wxpT = ctx.sb("wxpT", [128, 2, 96], BF16)
    wdtT = ctx.sb("wdtT", [64, 2, 128], BF16)
    bdt_r = ctx.sb("bdt_r", [1, E_SL], F32)
    cw = ctx.sb("cw", [128, 2, 4], F32)
    cb_c = ctx.sb("cb_c", [128, 2], F32)
    dd_c = ctx.sb("dd_c", [128, 2], F32)
    aneg = ctx.sb("aneg", [128, 2, D_STATE], F32)
    htok = ctx.sb("htok", [128, 4, D_MODEL], F32)
    hfm = ctx.sb("hfm", [128, 8, 512], BF16)
    r2row = ctx.sb("r2row", [1, tc], F32)
    r2rep = ctx.sb("r2rep", [128, 512], BF16)
    xcb = ctx.sb("xcb", [128, 2, tc + 3], BF16)
    xctail = ctx.sb("xctail", [128, 2, 3], BF16)
    resb = ctx.sb("resb", [128, 2, tc], BF16)
    acc0 = ctx.sb("acc0", [128, tc], F32)
    acc1 = ctx.sb("acc1", [128, tc], F32)
    xs = ctx.sb("xs", [128, 2, tc], BF16)
    dbcp = ctx.sb("dbcp", [96, tc], F32)
    dbc = ctx.sb("dbc", [96, tc], BF16)
    spA = ctx.sb("spA", [128, 512], F32)
    spR = ctx.sb("spR", [128, 512], F32)
    delta = ctx.sb("delta", [128, 2, tc], F32)
    deltab = ctx.sb("deltab", [128, 2, tc], BF16)
    du = ctx.sb("du", [128, 2, tc], BF16)
    dabuf = ctx.sb("dabuf", [128, tc], F32)
    dbubuf = ctx.sb("dbubuf", [128, tc], BF16)
    hbuf = ctx.sb("hbuf", [128, tc], BF16)
    zbuf = ctx.sb("zbuf", [128, tc], BF16)
    states = ctx.sb("states", [128, 32], F32)
    ytmp = ctx.sb("ytmp", [128, 512], BF16)
    ygb = ctx.sb("ygb", [128, 2, tc], BF16)
    nwrow = ctx.sb("nwrow", [1, D_MODEL], F32)
    wxpload = ctx.sb("wxpload", [96, E_SL], F32)
    wdtload = ctx.sb("wdtload", [128, 2, DT_RANK], F32)
    alogld = ctx.sb("alogld", [128, 2, D_STATE], F32)
    wld_cm = nc.sbuf_tensor("wld", [128, 2, D_MODEL], F32)
    wld = wld_cm.__enter__()

    emit_ident(sk, ctx, ident)
    sk.step("vector", lambda v: v.tensor_copy(identB[:], ident[:]))
    emit_ones(sk, ctx, onesr[:], 1.0)
    emit_ones(sk, ctx, onesrB[:], 1.0)
    def mksel(g):
        g.memset(selBC[:], 0.0)
        return g.affine_select(out=selBC[:], in_=selBC[:], compare_op=ALU.not_equal,
                               fill=1.0, base=-64, pattern=[[-1, 32], [0, 128]],
                               channel_multiplier=1)
    sk.step("gpsimd", mksel)

    sk.dma("sync", nwrow[:], norm_w)
    def trnw(t):
        for i in range(8):
            inst = t.transpose(pb[0][:, i:i + 1], nwrow[:, i * 128:(i + 1) * 128],
                               ident[0:1, 0:1])
        return inst
    sk.step("tensor", trnw)
    sk.step("vector", lambda v: v.tensor_copy(nwT[:], pb[0][:, 0:8]))
    for half, srcw in ((0, wi_xc), (1, wi_res)):
        sk.dma("sync", wld[:], srcw.rearrange("(i p) d -> p i d", p=128))
        for k in range(8):
            def mktr(k=k):
                def f(t):
                    for i in range(2):
                        inst = t.transpose(pb[i][:, 0:128],
                                           wld[:, i, k * 128:(k + 1) * 128], ident[:])
                    return inst
                return f
            sk.step("tensor", mktr())
            def mkev(k=k, half=half):
                def f(s):
                    for i in range(2):
                        inst = s.activation(
                            wiT[:, k, half * 256 + i * 128: half * 256 + (i + 1) * 128],
                            pb[i][:, 0:128], AF.Copy, scale=nwT[:, k:k + 1])
                    return inst
                return f
            sk.step("scalar", mkev())
    sk.dma("sync", wxpload[:], wxp)
    def trxp(t):
        for i in range(2):
            inst = t.transpose(pb[i][:, 0:96], wxpload[0:96, i * 128:(i + 1) * 128],
                               ident[0:96, 0:96])
        return inst
    sk.step("tensor", trxp)
    sk.step("vector", lambda v: (v.tensor_copy(wxpT[:, 0, :], pb[0][:, 0:96]),
                                 v.tensor_copy(wxpT[:, 1, :], pb[1][:, 0:96]))[-1])
    sk.dma("sync", wdtload[:], wdt.rearrange("(i p) r -> p i r", p=128))
    def trdt(t):
        for i in range(2):
            inst = t.transpose(pb[i][0:64, 0:128], wdtload[:, i, :], ident[:])
        return inst
    sk.step("tensor", trdt)
    sk.step("vector", lambda v: (v.tensor_copy(wdtT[0:64, 0, :], pb[0][0:64, 0:128]),
                                 v.tensor_copy(wdtT[0:64, 1, :], pb[1][0:64, 0:128]))[-1])
    sk.dma("sync", bdt_r[:], bdt[None, :])
    sk.dma("sync", cw[:], convw.rearrange("(i p) k -> p i k", p=128), slow=True)
    sk.dma("sync", cb_c[:], convb.rearrange("(i p) -> p i", p=128), slow=True)
    sk.dma("sync", dd_c[:], ddvec.rearrange("(i p) -> p i", p=128), slow=True)
    sk.dma("sync", alogld[:], alog.rearrange("(i p) n -> p i n", p=128), slow=True)
    sk.step("scalar", lambda s: s.activation(aneg[:], alogld[:], AF.Exp))
    sk.step("vector", lambda v: v.tensor_scalar(aneg[:], aneg[:], -1.0, None, ALU.mult))
    wld_cm.__exit__(None, None, None)

    for g in range(nchunk):
        bstart = (g % BCH == 0)
        col0 = g * tc
        if bstart:
            sk.step("vector", lambda v: v.memset(states[:], 0.0))
            sk.step("vector", lambda v: v.memset(xctail[:], 0.0))
        for tg in range(NTG):
            scol = col0 + tg * 512
            sk.dma("gpsimd", htok[:], h_full[scol:scol + 512, :].rearrange(
                "(tt p) d -> p tt d", p=128))
            for k in range(8):
                def mktr2(k=k):
                    def f(t):
                        for u in range(4):
                            inst = t.transpose(pb[0][:, u * 128:(u + 1) * 128],
                                               htok[:, u, k * 128:(k + 1) * 128], ident[:])
                        return inst
                    return f
                sk.step("tensor", mktr2())
                def mkev(k=k):
                    return lambda v: v.tensor_copy(hfm[:, k, :], pb[0][:, 0:512])
                sk.step("vector", mkev())
            sk.dma("sync", r2row[:, 0:512], r2_full[:, scol:scol + 512])
            sk.step("tensor", lambda t: t.matmul(pb[2][:], onesr[0:1, 0:128],
                                                 r2row[:, 0:512], start=True, stop=True))
            sk.step("vector", lambda v: v.tensor_copy(r2rep[:], pb[2][:]))
            for mt in range(4):
                def mkmm(mt=mt):
                    def f(t):
                        for k in range(8):
                            inst = t.matmul(pb[2 + (mt % 2)][:],
                                            wiT[:, k, mt * 128:(mt + 1) * 128],
                                            hfm[:, k, :],
                                            start=(k == 0), stop=(k == 7))
                        return inst
                    return f
                sk.step("tensor", mkmm())
                if mt < 2:
                    def mkev2(mt=mt, tg=tg):
                        return lambda v: v.tensor_tensor(
                            xcb[:, mt, 3 + tg * 512: 3 + tg * 512 + 512],
                            pb[2 + (mt % 2)][:], r2rep[:], ALU.mult)
                    sk.step("vector", mkev2())
                else:
                    def mkev2(mt=mt, tg=tg):
                        return lambda v: v.tensor_tensor(
                            resb[:, mt - 2, tg * 512:tg * 512 + 512],
                            pb[2 + (mt % 2)][:], r2rep[:], ALU.mult)
                    sk.step("vector", mkev2())
        sk.step("scalar", lambda s: s.activation(resb[:], resb[:], AF.Silu))
        sk.step("vector", lambda v: v.tensor_copy(xcb[:, :, 0:3], xctail[:]))
        for i in range(2):
            def mkcv(i=i):
                def f(v):
                    v.tensor_scalar(acc0[:], xcb[:, i, 0:tc], cw[:, i, 0:1], None, ALU.mult)
                    v.scalar_tensor_tensor(acc1[:], xcb[:, i, 1:tc + 1], cw[:, i, 1:2],
                                           acc0[:], ALU.mult, ALU.add)
                    v.scalar_tensor_tensor(acc0[:], xcb[:, i, 2:tc + 2], cw[:, i, 2:3],
                                           acc1[:], ALU.mult, ALU.add)
                    return v.scalar_tensor_tensor(acc1[:], xcb[:, i, 3:tc + 3], cw[:, i, 3:4],
                                                  acc0[:], ALU.mult, ALU.add)
                return f
            sk.step("vector", mkcv())
            def mksl(i=i):
                return lambda s: s.activation(xs[:, i, :], acc1[:], AF.Silu,
                                              bias=cb_c[:, i:i + 1])
            sk.step("scalar", mksl())
        sk.step("vector", lambda v: v.tensor_copy(xctail[:], xcb[:, :, tc:tc + 3]))
        for s in range(NS):
            def mkxp(s=s):
                def f(t):
                    for k in range(2):
                        inst = t.matmul(pb[3][0:96, :], wxpT[:, k, :],
                                        xs[:, k, s * 512:(s + 1) * 512],
                                        start=(k == 0), stop=(k == 1))
                    return inst
                return f
            sk.step("tensor", mkxp())
            def mkxe(s=s):
                return lambda v: v.tensor_copy(dbcp[:, s * 512:(s + 1) * 512], pb[3][0:96, :])
            sk.step("vector", mkxe())
        sk.dma("sync", cc_in, dbcp[:])
        sk.step("gpsimd", lambda gp: gp.collective_compute(
            "AllReduce", ALU.add, [list(range(8))], [cc_in], [cc_out]), weight=1)
        sk.dma("gpsimd", dbc[:], cc_out)
        for mt in range(2):
            for s in range(NS):
                def mkdt(mt=mt, s=s):
                    def f(t):
                        t.matmul(pb[2][:], wdtT[0:64, mt, :], dbc[0:64, s * 512:(s + 1) * 512],
                                 start=True, stop=False)
                        return t.matmul(pb[2][:], bdt_r[:, mt * 128:(mt + 1) * 128],
                                        onesr[:, 0:512], start=False, stop=True)
                    return f
                sk.step("tensor", mkdt())
                def mksp1(mt=mt, s=s):
                    return lambda v: v.tensor_scalar(spR[:], pb[2][:], 0.0, None, ALU.max)
                sk.step("vector", mksp1())
                def mksp2(mt=mt, s=s):
                    def f(s_):
                        s_.activation(spA[:], pb[2][:], AF.Abs)
                        s_.activation(spA[:], spA[:], AF.Exp, scale=-1.0)
                        return s_.activation(spA[:], spA[:], AF.Ln, bias=1.0)
                    return f
                sk.step("scalar", mksp2())
                def mksp3(mt=mt, s=s):
                    return lambda v: v.tensor_tensor(
                        delta[:, mt, s * 512:(s + 1) * 512], spA[:], spR[:], ALU.add)
                sk.step("vector", mksp3())
        sk.step("vector", lambda v: v.tensor_copy(deltab[:], delta[:]))
        sk.step("vector", lambda v: v.tensor_tensor(du[:], deltab[:], xs[:], ALU.mult))
        if debug and g == 0:
            sk.dma("sync", dbg_xs, xs[:])
            sk.dma("sync", dbg_delta, delta[:])
            sk.dma("sync", dbg_dbc, dbc[:])
            sk.dma("sync", dbg_res, resb[:])
            sk.dma("sync", dbg_xc, xcb[:])
        for n in range(D_STATE):
            def mkbc(n=n):
                def f(t):
                    for hh in range(NS):
                        t.matmul(psB[:, hh * 512:(hh + 1) * 512],
                                 selBC[:, n, :],
                                 dbc[:, hh * 512:(hh + 1) * 512],
                                 start=True, stop=True)
                        inst = t.matmul(psC[:, hh * 512:(hh + 1) * 512],
                                        selBC[:, 16 + n, :],
                                        dbc[:, hh * 512:(hh + 1) * 512],
                                        start=True, stop=True)
                    return inst
                return f
            sk.step("tensor", mkbc())
            for i in range(2):
                def mkda(i=i, n=n):
                    return lambda s: s.activation(dabuf[:], delta[:, i, :], AF.Exp,
                                                  scale=aneg[:, i, n:n + 1])
                sk.step("scalar", mkda())
                def mkdbu(i=i, n=n):
                    return lambda v: v.tensor_tensor(dbubuf[:], du[:, i, :], psB[:], ALU.mult)
                sk.step("vector", mkdbu())
                def mksc(i=i, n=n):
                    return lambda v: v.tensor_tensor_scan(
                        hbuf[:], dabuf[:], dbubuf[:], states[:, i * 16 + n:i * 16 + n + 1],
                        ALU.mult, ALU.add)
                sk.step("vector", mksc())
                def mkst(i=i, n=n):
                    return lambda v: v.tensor_copy(states[:, i * 16 + n:i * 16 + n + 1],
                                                   hbuf[:, tc - 1:tc])
                sk.step("vector", mkst())
                def mkz(i=i, n=n):
                    return lambda v: v.tensor_tensor(zbuf[:], hbuf[:], psC[:], ALU.mult)
                sk.step("vector", mkz())
                if debug and g == 0 and i == 0 and n == 0:
                    sk.dma("sync", dbg_da, dabuf[:])
                    sk.dma("sync", dbg_dbu, dbubuf[:])
                    sk.dma("sync", dbg_h, hbuf[:])
                    sk.dma("sync", dbg_z, zbuf[:])
                def mky(i=i, n=n):
                    def f(t):
                        for s in range(NS):
                            inst = t.matmul(pb[i * 2 + s][:], identB[:],
                                            zbuf[:, s * 512:(s + 1) * 512],
                                            start=(n == 0), stop=(n == 15))
                        return inst
                    return f
                sk.step("tensor", mky())
        for i in range(2):
            for s in range(NS):
                def mkye(i=i, s=s):
                    return lambda v: v.scalar_tensor_tensor(
                        ytmp[:],
                        xs[:, i, s * 512:(s + 1) * 512], dd_c[:, i:i + 1],
                        pb[i * 2 + s][:], ALU.mult, ALU.add)
                sk.step("vector", mkye())
                def mkyg(i=i, s=s):
                    return lambda v: v.tensor_tensor(
                        ygb[:, i, s * 512:(s + 1) * 512], ytmp[:],
                        resb[:, i, s * 512:(s + 1) * 512], ALU.mult)
                sk.step("vector", mkyg())
                if debug and g == 0:
                    def mkdy(i=i, s=s):
                        return lambda e: e.dma_start(
                            out=dbg_y[:, i, s * 512:(s + 1) * 512], in_=ytmp[:])
                    sk.step("sync", mkdy(), weight=16)
        sk.dma("sync", yg_out.rearrange("(i p) t -> p i t", p=128)[:, :, col0:col0 + tc],
               ygb[:])

    v = sk.val
    sk.plan["sync"].append((v, lambda e: e.nop(), 1))
    blk = ctx.es.enter_context(nc.Block())
    sk.emit_all(blk)
    ctx.es.close()
    return nc



def build_L3(tch=512):
    nc = bass.Bass("TRN2", target_bir_lowering=False)
    NCH = T_OWN // tch
    yg = nc.dram_tensor("yg_own", [D_INNER, T_OWN], BF16, kind="ExternalInput").ap()
    h_own = nc.dram_tensor("h_own", [T_OWN, D_MODEL], BF16, kind="ExternalInput").ap()
    x_own = nc.dram_tensor("x_own", [T_OWN, D_MODEL], F32, kind="ExternalInput").ap()
    wout = nc.dram_tensor("wout", [D_MODEL, D_INNER], F32, kind="ExternalInput").ap()
    c_sel = nc.dram_tensor("c_sel", [1, D_MODEL], F32, kind="ExternalInput").ap()
    wmod = nc.dram_tensor("wmod", [D_MODEL, 6 * D_MODEL], F32, kind="ExternalInput").ap()
    xmid_o = nc.dram_tensor("xmid", [D_MODEL, T_OWN], BF16, kind="ExternalOutput").ap()
    h2_o = nc.dram_tensor("h2", [D_MODEL, T_OWN], BF16, kind="ExternalOutput").ap()
    gmax_o = nc.dram_tensor("gmax", [1, 16], F32, kind="ExternalOutput").ap()
    cc_in = nc.dram_tensor("cc_in", [1, 16], F32, kind="Internal").ap()
    cc_out = nc.dram_tensor("cc_out", [1, 16], F32, kind="Internal").ap()

    ctx = Ctx(nc)
    sk = Sched(ctx)
    pb = [ctx.ps(f"pb{i}", [128, 512], F32) for i in range(4)]
    pr = ctx.ps("pr", [128, 512], F32)
    ps1 = ctx.ps("ps1", [1, 512], F32)
    ps2 = ctx.ps("ps2", [1, 512], F32)

    ident = ctx.sb("ident", [128, 128], F32)
    onescolB = ctx.sb("onescolB", [128, 1], BF16)
    onesr = ctx.sb("onesr", [1, 128], F32)
    csilu = ctx.sb("csilu", [1, D_MODEL], F32)
    csiluT = ctx.sb("csiluT", [128, 8], F32)
    row_ga = ctx.sb("row_ga", [1, D_MODEL], F32)
    row_sb = ctx.sb("row_sb", [1, D_MODEL], F32)
    row_cb = ctx.sb("row_cb", [1, D_MODEL], F32)
    rtmp = ctx.sb("rtmp", [1, D_MODEL], F32)
    rtmp2 = ctx.sb("rtmp2", [1, D_MODEL], F32)
    gaT = ctx.sb("gaT", [128, 8], F32)
    cbT1 = ctx.sb("cbT1", [128, 8], F32)
    sbT = ctx.sb("sbT", [128, 8], F32)
    woT = ctx.sb("woT", [128, 16, 1024], BF16)
    ygc = ctx.sb("ygc", [128, 16, tch], BF16)
    xtok = ctx.sb("xtok", [128, 4, D_MODEL], F32)
    htok = ctx.sb("htok", [128, 4, D_MODEL], F32)
    x_fm = ctx.sb("x_fm", [128, 8, tch], BF16)
    h_fm = ctx.sb("h_fm", [128, 8, tch], BF16)
    t1b = ctx.sb("t1b", [128, tch], BF16)
    xmid = ctx.sb("xmid_s", [128, 8, tch], BF16)
    xsq = ctx.sb("xsq", [128, 8, tch], BF16)
    h2b = ctx.sb("h2b", [128, 8, tch], BF16)
    mrow = ctx.sb("mrow", [1, tch], F32)
    rrow = ctx.sb("rrow", [1, tch], F32)
    mrep = ctx.sb("mrep", [128, tch], BF16)
    rrep = ctx.sb("rrep", [128, tch], BF16)
    mc = ctx.sb("mc", [128, 8 * NCH], F32)
    maxc = ctx.sb("maxc", [128, 1], F32)
    maxr = ctx.sb("maxr", [1, 16], F32)
    swk1 = ctx.sb("swk1", [128, 1024], F32)
    wldc = nc.sbuf_tensor("wldc", [128, 4, D_INNER], F32)
    wld = wldc.__enter__()

    emit_ident(sk, ctx, ident)
    emit_ones(sk, ctx, onescolB[:], 1.0)
    emit_ones(sk, ctx, onesr[:], 1.0)
    sk.dma("sync", csilu[:], c_sel)
    emit_mod_rows(sk, ctx, c_sel=csilu, swmod_hbm=wmod, slots=[2, 3, 4],
                  row_out={2: row_ga, 3: row_sb, 4: row_cb}, csilu=rtmp2,
                  csiluT=csiluT, ident=ident, pst=pr[:, 0:8],
                  psrow0=pb[0][0:1, 0:512], psrow1=pb[1][0:1, 0:512], swk=swk1)
    sk.step("scalar", lambda s: s.activation(rtmp[:], row_ga[:], AF.Sign))
    sk.step("scalar", lambda s: s.activation(row_ga[:], row_ga[:], AF.Abs))
    sk.step("vector", lambda v: v.tensor_scalar(row_ga[:], row_ga[:], 0.25, None, ALU.max))
    sk.step("vector", lambda v: v.tensor_tensor(row_ga[:], row_ga[:], rtmp[:], ALU.mult))
    sk.step("vector", lambda v: v.tensor_scalar(row_cb[:], row_cb[:], 1.0, None, ALU.add))
    for row, colT in ((row_ga, gaT), (row_cb, cbT1), (row_sb, sbT)):
        def mktr(row=row, colT=colT):
            def f(t):
                for i in range(8):
                    inst = t.transpose(pr[:, i:i + 1], row[:, i * 128:(i + 1) * 128],
                                       ident[0:1, 0:1])
                return inst
            return f
        sk.step("tensor", mktr())
        def mkc(colT=colT):
            return lambda v: v.tensor_copy(colT[:], pr[:, 0:8])
        sk.step("vector", mkc())
    for hf in range(2):
        for mi in range(4):
            sk.dma("sync", wld[:, mi, :], wout[(hf * 4 + mi) * 128:(hf * 4 + mi + 1) * 128, :])
        for mi in range(4):
            for kt in range(16):
                def mktr(mi=mi, kt=kt):
                    return lambda t: t.transpose(pb[kt % 2][:, 0:128],
                                                 wld[:, mi, kt * 128:(kt + 1) * 128], ident[:])
                sk.step("tensor", mktr())
                def mkev(hf=hf, mi=mi, kt=kt):
                    return lambda v: v.tensor_copy(
                        woT[:, kt, (hf * 4 + mi) * 128:(hf * 4 + mi + 1) * 128],
                        pb[kt % 2][:, 0:128])
                sk.step("vector", mkev())
    wldc.__exit__(None, None, None)

    for ch in range(NCH):
        col0 = ch * tch
        sk.dma("sync", ygc[:], yg[:, col0:col0 + tch].rearrange("(i p) t -> p i t", p=128))
        sk.dma("gpsimd", xtok[:], x_own[col0:col0 + tch, :].rearrange(
            "(tt p) d -> p tt d", p=128))
        sk.dma("gpsimd", htok[:], h_own[col0:col0 + tch, :].rearrange(
            "(tt p) d -> p tt d", p=128))
        for src_, dst in ((xtok, x_fm), (htok, h_fm)):
            for k in range(8):
                def mktr(src_=src_, k=k):
                    def f(t):
                        for u in range(4):
                            inst = t.transpose(pb[0][:, u * 128:(u + 1) * 128],
                                               src_[:, u, k * 128:(k + 1) * 128], ident[:])
                        return inst
                    return f
                sk.step("tensor", mktr())
                def mkev(dst=dst, k=k):
                    return lambda v: v.tensor_copy(dst[:, k, :], pb[0][:, 0:512])
                sk.step("vector", mkev())
        for mt in range(8):
            def mkmm(mt=mt):
                def f(t):
                    for kt in range(16):
                        inst = t.matmul(pb[1 + (mt % 2)][:], woT[:, kt, mt * 128:(mt + 1) * 128],
                                        ygc[:, kt, :], start=(kt == 0), stop=(kt == 15))
                    return inst
                return f
            sk.step("tensor", mkmm())
            def mkt1(mt=mt):
                return lambda v: v.tensor_tensor(t1b[:], pb[1 + (mt % 2)][:],
                                                 h_fm[:, mt, :], ALU.add)
            sk.step("vector", mkt1())
            def mkxm(mt=mt):
                return lambda v: v.scalar_tensor_tensor(
                    xmid[:, mt, :], t1b[:], gaT[:, mt:mt + 1], x_fm[:, mt, :],
                    ALU.mult, ALU.add)
            sk.step("vector", mkxm())
        sk.dma("sync", xmid_o.rearrange("(i p) t -> p i t", p=128)[:, :, col0:col0 + tch],
               xmid[:])
        sk.step("scalar", lambda s: s.activation(xsq[:], xmid[:], AF.Square))
        def mkst(t):
            for kt in range(8):
                t.matmul(ps1[:], onescolB[:], xmid[:, kt, :], start=(kt == 0), stop=(kt == 7))
                inst = t.matmul(ps2[:], onescolB[:], xsq[:, kt, :], start=(kt == 0), stop=(kt == 7))
            return inst
        sk.step("tensor", mkst)
        def mkstat(v):
            v.tensor_scalar(mrow[:], ps1[:], 1.0 / D_MODEL, None, ALU.mult)
            v.tensor_scalar(rrow[:], ps2[:], 1.0 / D_MODEL, None, ALU.mult)
            v.tensor_tensor(rtmp[:, 0:tch], mrow[:], mrow[:], ALU.mult)
            v.tensor_tensor(rrow[:], rrow[:], rtmp[:, 0:tch], ALU.subtract)
            v.tensor_scalar(rrow[:], rrow[:], EPS, None, ALU.add)
            return v.reciprocal(rrow[:], rrow[:])
        sk.step("vector", mkstat)
        sk.step("scalar", lambda s: s.activation(rrow[:], rrow[:], AF.Sqrt))
        for row, rep in ((mrow, mrep), (rrow, rrep)):
            def mkrp(row=row):
                return lambda t: t.matmul(pr[:], onesr[0:1, :], row[:], start=True, stop=True)
            sk.step("tensor", mkrp())
            def mkrc(rep=rep):
                return lambda v: v.tensor_copy(rep[:], pr[:])
            sk.step("vector", mkrc())
        for mt in range(8):
            def mkh2(mt=mt, ch=ch):
                def f(v):
                    v.tensor_tensor(t1b[:], xmid[:, mt, :], mrep[:], ALU.subtract)
                    v.tensor_tensor(t1b[:], t1b[:], rrep[:], ALU.mult)
                    v.tensor_scalar(h2b[:, mt, :], t1b[:], cbT1[:, mt:mt + 1],
                                    sbT[:, mt:mt + 1], ALU.mult, ALU.add)
                    return v.tensor_reduce(mc[:, ch * 8 + mt:ch * 8 + mt + 1],
                                           h2b[:, mt, :], mybir.AxisListType.X, ALU.max)
                return f
            sk.step("vector", mkh2())
        sk.dma("sync", h2_o.rearrange("(i p) t -> p i t", p=128)[:, :, col0:col0 + tch],
               h2b[:])
    sk.step("vector", lambda v: v.tensor_reduce(maxc[:], mc[:], mybir.AxisListType.X, ALU.max))
    sk.step("tensor", lambda t: t.transpose(pr[0:1, 0:128], maxc[:], ident[:]))
    sk.step("vector", lambda v: v.tensor_reduce(maxr[:, 0:1], pr[0:1, 0:128],
                                                mybir.AxisListType.X, ALU.max))
    sk.step("vector", lambda v: v.tensor_scalar(maxr[:], onesr[0:1, 0:16],
                                                maxr[0:1, 0:1], None, ALU.mult))
    sk.dma("sync", cc_in, maxr[:])
    sk.step("gpsimd", lambda gp: gp.collective_compute(
        "AllReduce", ALU.max, [list(range(8))], [cc_in], [cc_out]), weight=1)
    sk.dma("sync", maxr[:], cc_out)
    sk.dma("sync", gmax_o, maxr[:])

    v = sk.val
    sk.plan["sync"].append((v, lambda e: e.nop(), 1))
    blk = ctx.es.enter_context(nc.Block())
    sk.emit_all(blk)
    ctx.es.close()
    return nc



def build_L4(tch=512):
    nc = bass.Bass("TRN2", target_bir_lowering=False)
    NCH = T_OWN // tch
    h2i = nc.dram_tensor("h2_own", [D_MODEL, T_OWN], BF16, kind="ExternalInput").ap()
    gmax_i = nc.dram_tensor("gmax", [1, 16], F32, kind="ExternalInput").ap()
    w1 = nc.dram_tensor("w1", [DFF, D_MODEL], F32, kind="ExternalInput").ap()
    b1 = nc.dram_tensor("b1", [DFF], F32, kind="ExternalInput").ap()
    h1_o = nc.dram_tensor("h1", [DFF, T_OWN], BF16, kind="ExternalOutput").ap()
    gmax2_o = nc.dram_tensor("gmax2", [1, 16], F32, kind="ExternalOutput").ap()
    d4_i1 = nc.dram_tensor("d4_i1", [128, 8, 512], BF16, kind="ExternalOutput").ap()
    d4_sv = nc.dram_tensor("d4_sv", [128, 32], F32, kind="ExternalOutput").ap()
    d4_bq = nc.dram_tensor("d4_bq", [128, 32], F32, kind="ExternalOutput").ap()
    d4_sw = nc.dram_tensor("d4_sw", [128, 32], F32, kind="ExternalOutput").ap()
    d4_w1 = nc.dram_tensor("d4_w1", [128, 1024], BF16, kind="ExternalOutput").ap()
    cc_in = nc.dram_tensor("cc_in", [1, 16], F32, kind="Internal").ap()
    cc_out = nc.dram_tensor("cc_out", [1, 16], F32, kind="Internal").ap()

    ctx = Ctx(nc)
    sk = Sched(ctx)
    pb = [ctx.ps(f"pb{i}", [128, 512], F32) for i in range(4)]
    pr = ctx.ps("pr", [128, 512], F32)

    ident = ctx.sb("ident", [128, 128], F32)
    onesr = ctx.sb("onesr", [1, 128], F32)
    w1qT = ctx.sb("w1qT", [128, 8, DFF], BF16)
    swc = ctx.sb("swc", [128, 32], F32)
    bqc = ctx.sb("bqc", [128, 32], F32)
    svc = ctx.sb("svc", [128, 32], F32)
    sinb = ctx.sb("sinb", [128, 1], F32)
    rsinb = ctx.sb("rsinb", [128, 1], F32)
    gmaxs = ctx.sb("gmaxs", [1, 16], F32)
    h2c = ctx.sb("h2c", [128, 8, tch], BF16)
    i1i = ctx.sb("i1i", [128, 8, tch], I32)
    i1b = ctx.sb("i1b", [128, 8, tch], BF16)
    h1c = ctx.sb("h1c", [128, tch], BF16)
    mc = ctx.sb("mc", [128, 32], F32)
    maxc = ctx.sb("maxc", [128, 1], F32)
    maxr = ctx.sb("maxr", [1, 16], F32)
    b1c = ctx.sb("b1c", [128, 32], F32)
    bti = ctx.sb("bti", [128, 32], I32)
    btf = ctx.sb("btf", [128, 32], F32)
    wl_cm = nc.sbuf_tensor("wl", [128, D_MODEL], F32)
    wl = wl_cm.__enter__()
    wq_cm = nc.sbuf_tensor("wq", [128, D_MODEL], I32)
    wq = wq_cm.__enter__()
    wqf_cm = nc.sbuf_tensor("wqf", [128, D_MODEL], F32)
    wqf = wqf_cm.__enter__()

    emit_ident(sk, ctx, ident)
    emit_ones(sk, ctx, onesr[:], 1.0)
    sk.dma("sync", gmaxs[:], gmax_i)
    sk.dma("sync", b1c[:], b1.rearrange("(i p) -> p i", p=128), slow=True)
    sk.step("vector", lambda v: v.tensor_scalar(gmaxs[:, 0:1], gmaxs[:, 0:1], 1e-8,
                                                None, ALU.max))
    sk.step("vector", lambda v: v.tensor_scalar(gmaxs[:, 0:1], gmaxs[:, 0:1],
                                                1.0 / 255.0, None, ALU.mult))
    sk.step("tensor", lambda t: t.matmul(pr[:, 0:1], onesr[:], gmaxs[0:1, 0:1],
                                         start=True, stop=True))
    sk.step("vector", lambda v: v.tensor_copy(sinb[:], pr[:, 0:1]))
    sk.step("vector", lambda v: v.reciprocal(rsinb[:], sinb[:]))
    for fi in range(32):
        sk.dma("sync", wl[:], w1[fi * 128:(fi + 1) * 128, :])
        sk.step("scalar", lambda s: s.activation(wqf[:], wl[:], AF.Abs))
        def mkr1(fi=fi):
            return lambda v: v.tensor_reduce(swc[:, fi:fi + 1], wqf[:],
                                             mybir.AxisListType.X, ALU.max)
        sk.step("vector", mkr1())
        def mkr2(fi=fi):
            return lambda v: v.tensor_scalar(swc[:, fi:fi + 1], swc[:, fi:fi + 1],
                                             1e-8, None, ALU.max)
        sk.step("vector", mkr2())
        def mkr3(fi=fi):
            return lambda v: v.tensor_scalar(swc[:, fi:fi + 1], swc[:, fi:fi + 1],
                                             1.0 / 127.0, None, ALU.mult)
        sk.step("vector", mkr3())
        def mkr4(fi=fi):
            return lambda v: v.reciprocal(mc[:, 0:1], swc[:, fi:fi + 1])
        sk.step("vector", mkr4())
        def mkq2(fi=fi):
            def f(v):
                v.tensor_scalar(wq[:], wl[:], mc[:, 0:1], None, ALU.mult)
                v.tensor_scalar(wq[:], wq[:], -128, 127, ALU.max, ALU.min)
                return v.tensor_copy(wqf[:], wq[:])
            return f
        sk.step("vector", mkq2())
        for kt in range(8):
            def mktr(fi=fi, kt=kt):
                return lambda t: t.transpose(pb[kt % 2][:, 0:128],
                                             wqf[:, kt * 128:(kt + 1) * 128], ident[:])
            sk.step("tensor", mktr())
            def mkev(fi=fi, kt=kt):
                return lambda v: v.tensor_copy(w1qT[:, kt, fi * 128:(fi + 1) * 128],
                                               pb[kt % 2][:, 0:128])
            sk.step("vector", mkev())
    sk.step("vector", lambda v: v.tensor_scalar(svc[:], swc[:], sinb[:, 0:1], None, ALU.mult))
    sk.step("vector", lambda v: v.reciprocal(mc[:, 0:32], svc[:]))
    sk.step("vector", lambda v: v.tensor_tensor(bti[:], b1c[:], mc[:, 0:32], ALU.mult))
    sk.step("vector", lambda v: v.tensor_copy(btf[:], bti[:]))
    sk.step("vector", lambda v: v.tensor_tensor(bqc[:], btf[:], svc[:], ALU.mult))
    wqf_cm.__exit__(None, None, None)
    wq_cm.__exit__(None, None, None)
    wl_cm.__exit__(None, None, None)

    sk.dma("sync", d4_sv, svc[:])
    sk.dma("sync", d4_bq, bqc[:])
    sk.dma("sync", d4_sw, swc[:])
    sk.dma("sync", d4_w1, w1qT[:, 0, 0:1024])
    for ch in range(NCH):
        col0 = ch * tch
        sk.dma("sync", h2c[:], h2i[:, col0:col0 + tch].rearrange("(i p) t -> p i t", p=128))
        def mki1(v):
            v.tensor_scalar(i1i[:], h2c[:], rsinb[:, 0:1], None, ALU.mult)
            v.tensor_scalar(i1i[:], i1i[:], 0, 255, ALU.max, ALU.min)
            return v.tensor_copy(i1b[:], i1i[:])
        sk.step("vector", mki1)
        if ch == 0:
            sk.dma("sync", d4_i1, i1b[:])
        for mt in range(32):
            def mkmm(mt=mt):
                def f(t):
                    for kt in range(8):
                        inst = t.matmul(pb[mt % 2][:], w1qT[:, kt, mt * 128:(mt + 1) * 128],
                                        i1b[:, kt, :], start=(kt == 0), stop=(kt == 7))
                    return inst
                return f
            sk.step("tensor", mkmm())
            def mkev(mt=mt, ch=ch):
                return lambda s: s.activation(h1c[:], pb[mt % 2][:], AF.Relu,
                                              bias=bqc[:, mt:mt + 1],
                                              scale=svc[:, mt:mt + 1])
            sk.step("scalar", mkev())
            def mkmx(mt=mt, ch=ch):
                return lambda v: v.tensor_reduce(mc[:, mt:mt + 1] if ch == 0 else mc[:, mt:mt + 1],
                                                 h1c[:], mybir.AxisListType.X, ALU.max)
            def mkmxa(mt=mt):
                return lambda v: v.tensor_reduce(maxc[:], h1c[:],
                                                 mybir.AxisListType.X, ALU.max)
            sk.step("vector", mkmxa())
            if ch == 0:
                def mkmxb(mt=mt):
                    return lambda v: v.tensor_copy(mc[:, mt:mt + 1], maxc[:])
            else:
                def mkmxb(mt=mt):
                    return lambda v: v.tensor_tensor(mc[:, mt:mt + 1],
                                                     mc[:, mt:mt + 1], maxc[:], ALU.max)
            sk.step("vector", mkmxb())
            def mkdm(mt=mt, col0=col0):
                return lambda e: e.dma_start(
                    out=h1_o[mt * 128:(mt + 1) * 128, col0:col0 + tch], in_=h1c[:])
            sk.step("sync", mkdm(), weight=16)
    sk.step("vector", lambda v: v.tensor_reduce(maxc[:], mc[:], mybir.AxisListType.X, ALU.max))
    sk.step("tensor", lambda t: t.transpose(pr[0:1, 0:128], maxc[:], ident[:]))
    sk.step("vector", lambda v: v.tensor_reduce(maxr[:, 0:1], pr[0:1, 0:128],
                                                mybir.AxisListType.X, ALU.max))
    sk.step("vector", lambda v: v.tensor_scalar(maxr[:], onesr[0:1, 0:16],
                                                maxr[0:1, 0:1], None, ALU.mult))
    sk.dma("sync", cc_in, maxr[:])
    sk.step("gpsimd", lambda gp: gp.collective_compute(
        "AllReduce", ALU.max, [list(range(8))], [cc_in], [cc_out]), weight=1)
    sk.dma("sync", maxr[:], cc_out)
    sk.dma("sync", gmax2_o, maxr[:])

    v = sk.val
    sk.plan["sync"].append((v, lambda e: e.nop(), 1))
    blk = ctx.es.enter_context(nc.Block())
    sk.emit_all(blk)
    ctx.es.close()
    return nc



def build_L5(tch=512):
    nc = bass.Bass("TRN2", target_bir_lowering=False)
    NCH = T_OWN // tch
    h1i = nc.dram_tensor("h1_own", [DFF, T_OWN], BF16, kind="ExternalInput").ap()
    gmax_i = nc.dram_tensor("gmax2", [1, 16], F32, kind="ExternalInput").ap()
    w2 = nc.dram_tensor("w2", [D_MODEL, DFF], F32, kind="ExternalInput").ap()
    b2 = nc.dram_tensor("b2", [D_MODEL], F32, kind="ExternalInput").ap()
    xmid_i = nc.dram_tensor("xmid", [D_MODEL, T_OWN], BF16, kind="ExternalInput").ap()
    c_sel = nc.dram_tensor("c_sel", [1, D_MODEL], F32, kind="ExternalInput").ap()
    wmod = nc.dram_tensor("wmod", [D_MODEL, 6 * D_MODEL], F32, kind="ExternalInput").ap()
    out_o = nc.dram_tensor("outp", [D_MODEL, T_OWN], F32, kind="ExternalOutput").ap()

    ctx = Ctx(nc)
    sk = Sched(ctx)
    pb = [ctx.ps(f"pb{i}", [128, 512], F32) for i in range(7)]
    pr = ctx.ps("pr", [128, 512], F32)
    bank = lambda mt: pb[mt] if mt < 7 else pr

    ident = ctx.sb("ident", [128, 128], F32)
    onesr = ctx.sb("onesr", [1, 128], F32)
    csilu = ctx.sb("csilu", [1, D_MODEL], F32)
    csiluT = ctx.sb("csiluT", [128, 8], F32)
    row_gb = ctx.sb("row_gb", [1, D_MODEL], F32)
    rtmp = ctx.sb("rtmp", [1, D_MODEL], F32)
    rtmp2 = ctx.sb("rtmp2", [1, D_MODEL], F32)
    gbT = ctx.sb("gbT", [128, 8], F32)
    w2qT = ctx.sb("w2qT", [128, 32, D_MODEL], BF16)
    swc = ctx.sb("swc", [128, 8], F32)
    bqc = ctx.sb("bqc", [128, 8], F32)
    svc = ctx.sb("svc", [128, 8], F32)
    sinb = ctx.sb("sinb", [128, 1], F32)
    rsinb = ctx.sb("rsinb", [128, 1], F32)
    gmaxs = ctx.sb("gmaxs", [1, 16], F32)
    h1c = ctx.sb("h1c", [128, 16, tch], BF16)
    i2i = ctx.sb("i2i", [128, 8, tch], I32)
    i2b = ctx.sb("i2b", [128, 16, tch], BF16)
    xmc = ctx.sb("xmc", [128, 8, tch], BF16)
    f2c = ctx.sb("f2c", [128, tch], BF16)
    outc = ctx.sb("outc", [128, 8, tch], F32)
    b2c = ctx.sb("b2c", [128, 8], F32)
    bti = ctx.sb("bti", [128, 8], I32)
    btf = ctx.sb("btf", [128, 8], F32)
    mcq = ctx.sb("mcq", [128, 8], F32)
    gbq = ctx.sb("gbq", [128, 8], F32)
    swk1 = ctx.sb("swk1", [128, 1024], F32)
    wl_cm = nc.sbuf_tensor("wl", [128, 2, DFF // 2], F32)
    wl = wl_cm.__enter__()
    wq_cm = nc.sbuf_tensor("wq", [128, DFF // 2], I32)
    wq = wq_cm.__enter__()
    wqf_cm = nc.sbuf_tensor("wqf", [128, DFF // 2], F32)
    wqf = wqf_cm.__enter__()

    emit_ident(sk, ctx, ident)
    emit_ones(sk, ctx, onesr[:], 1.0)
    sk.dma("sync", gmaxs[:], gmax_i)
    sk.dma("sync", b2c[:], b2.rearrange("(i p) -> p i", p=128), slow=True)
    sk.dma("sync", csilu[:], c_sel)
    emit_mod_rows(sk, ctx, c_sel=csilu, swmod_hbm=wmod, slots=[5],
                  row_out={5: row_gb}, csilu=rtmp2, csiluT=csiluT, ident=ident,
                  pst=pr[:, 0:8], psrow0=pb[0][0:1, 0:512], psrow1=pb[1][0:1, 0:512],
                  swk=swk1)
    sk.step("scalar", lambda s: s.activation(rtmp[:], row_gb[:], AF.Sign))
    sk.step("scalar", lambda s: s.activation(row_gb[:], row_gb[:], AF.Abs))
    sk.step("vector", lambda v: v.tensor_scalar(row_gb[:], row_gb[:], 0.25, None, ALU.max))
    sk.step("vector", lambda v: v.tensor_tensor(row_gb[:], row_gb[:], rtmp[:], ALU.mult))
    def mktr(t):
        for i in range(8):
            inst = t.transpose(pr[:, i:i + 1], row_gb[:, i * 128:(i + 1) * 128],
                               ident[0:1, 0:1])
        return inst
    sk.step("tensor", mktr)
    sk.step("vector", lambda v: v.tensor_copy(gbT[:], pr[:, 0:8]))
    sk.step("vector", lambda v: v.tensor_scalar(gmaxs[:, 0:1], gmaxs[:, 0:1], 1e-8,
                                                None, ALU.max))
    sk.step("vector", lambda v: v.tensor_scalar(gmaxs[:, 0:1], gmaxs[:, 0:1],
                                                1.0 / 255.0, None, ALU.mult))
    sk.step("tensor", lambda t: t.matmul(pr[:, 0:1], onesr[:], gmaxs[0:1, 0:1],
                                         start=True, stop=True))
    sk.step("vector", lambda v: v.tensor_copy(sinb[:], pr[:, 0:1]))
    sk.step("vector", lambda v: v.reciprocal(rsinb[:], sinb[:]))
    for di in range(8):
        sk.dma("sync", wl[:], w2[di * 128:(di + 1) * 128, :].rearrange(
            "p (hh f) -> p hh f", hh=2))
        def mkabs(di=di):
            def f(s):
                return s.activation(wqf[:], wl[:, 0, :], AF.Abs)
            return f
        sk.step("scalar", mkabs())
        sk.step("vector", lambda v: v.tensor_reduce(mcq[:, 1:2], wqf[:],
                                                    mybir.AxisListType.X, ALU.max))
        sk.step("scalar", lambda s: s.activation(wqf[:], wl[:, 1, :], AF.Abs))
        sk.step("vector", lambda v: v.tensor_reduce(mcq[:, 2:3], wqf[:],
                                                    mybir.AxisListType.X, ALU.max))
        def mkq1(di=di):
            return lambda v: v.tensor_tensor(swc[:, di:di + 1], mcq[:, 1:2],
                                             mcq[:, 2:3], ALU.max)
        sk.step("vector", mkq1())
        def mkq1b(di=di):
            return lambda v: v.tensor_scalar(swc[:, di:di + 1], swc[:, di:di + 1],
                                             1e-8, None, ALU.max)
        sk.step("vector", mkq1b())
        def mkq1c(di=di):
            return lambda v: v.tensor_scalar(swc[:, di:di + 1], swc[:, di:di + 1],
                                             1.0 / 127.0, None, ALU.mult)
        sk.step("vector", mkq1c())
        def mkq1d(di=di):
            return lambda v: v.reciprocal(mcq[:, 0:1], swc[:, di:di + 1])
        sk.step("vector", mkq1d())
        for hh in range(2):
            def mkq2(di=di, hh=hh):
                def f(v):
                    v.tensor_scalar(wq[:], wl[:, hh, :], mcq[:, 0:1], None, ALU.mult)
                    v.tensor_scalar(wq[:], wq[:], -128, 127, ALU.max, ALU.min)
                    return v.tensor_copy(wqf[:], wq[:])
                return f
            sk.step("vector", mkq2())
            for kt in range(16):
                def mktrq(di=di, hh=hh, kt=kt):
                    return lambda t: t.transpose(pb[kt % 2][:, 0:128],
                                                 wqf[:, kt * 128:(kt + 1) * 128], ident[:])
                sk.step("tensor", mktrq())
                def mkev(di=di, hh=hh, kt=kt):
                    return lambda v: v.tensor_copy(
                        w2qT[:, hh * 16 + kt, di * 128:(di + 1) * 128],
                        pb[kt % 2][:, 0:128])
                sk.step("vector", mkev())
    sk.step("vector", lambda v: v.tensor_scalar(svc[:], swc[:], sinb[:, 0:1], None, ALU.mult))
    sk.step("vector", lambda v: v.reciprocal(mcq[:], svc[:]))
    sk.step("vector", lambda v: v.tensor_tensor(bti[:], b2c[:], mcq[:], ALU.mult))
    sk.step("vector", lambda v: v.tensor_copy(btf[:], bti[:]))
    sk.step("vector", lambda v: v.tensor_tensor(bqc[:], btf[:], svc[:], ALU.mult))
    sk.step("vector", lambda v: v.tensor_tensor(gbq[:], gbT[:], bqc[:], ALU.mult))
    wqf_cm.__exit__(None, None, None)
    wq_cm.__exit__(None, None, None)
    wl_cm.__exit__(None, None, None)

    for ch in range(NCH):
        col0 = ch * tch
        sk.dma("sync", xmc[:], xmid_i[:, col0:col0 + tch].rearrange("(i p) t -> p i t", p=128))
        for half in range(2):
            def mkd(half=half, col0=col0):
                return lambda e: e.dma_start(
                    out=h1c[:],
                    in_=h1i[half * 2048:(half + 1) * 2048, col0:col0 + tch].rearrange(
                        "(i p) t -> p i t", p=128))
            sk.step("sync", mkd(), weight=16)
            for grp in range(2):
                def mki2(grp=grp):
                    def f(v):
                        v.tensor_scalar(i2i[:], h1c[:, grp * 8:(grp + 1) * 8, :],
                                        rsinb[:, 0:1], None, ALU.mult)
                        v.tensor_scalar(i2i[:], i2i[:], 0, 255, ALU.max, ALU.min)
                        return v.tensor_copy(i2b[:, grp * 8:(grp + 1) * 8, :], i2i[:])
                    return f
                sk.step("vector", mki2(grp))
            for mt in range(8):
                def mkmm(mt=mt, half=half):
                    def f(t):
                        for kt in range(16):
                            inst = t.matmul(bank(mt)[:],
                                            w2qT[:, half * 16 + kt, mt * 128:(mt + 1) * 128],
                                            i2b[:, kt, :],
                                            start=(half == 0 and kt == 0),
                                            stop=(half == 1 and kt == 15))
                        return inst
                    return f
                sk.step("tensor", mkmm())
        for mt in range(8):
            def mkev(mt=mt):
                return lambda s: s.activation(f2c[:], bank(mt)[:], AF.Copy,
                                              scale=svc[:, mt:mt + 1])
            sk.step("scalar", mkev())
            def mkout(mt=mt):
                return lambda v: v.scalar_tensor_tensor(
                    outc[:, mt, :], f2c[:], gbT[:, mt:mt + 1], xmc[:, mt, :],
                    ALU.mult, ALU.add)
            sk.step("vector", mkout())
            def mkout2(mt=mt):
                return lambda v: v.tensor_scalar(outc[:, mt, :], outc[:, mt, :],
                                                 gbq[:, mt:mt + 1], None, ALU.add)
            sk.step("vector", mkout2())
        sk.dma("sync", out_o.rearrange("(i p) t -> p i t", p=128)[:, :, col0:col0 + tch],
               outc[:])

    v = sk.val
    sk.plan["sync"].append((v, lambda e: e.nop(), 1))
    blk = ctx.es.enter_context(nc.Block())
    sk.emit_all(blk)
    ctx.es.close()
    return nc



_NC_CACHE = {}


def _get(name, builder):
    if name not in _NC_CACHE:
        _NC_CACHE[name] = builder()
    return _NC_CACHE[name]


def kernel(L0_attn_x, L0_attn_W_in, L0_attn_conv_w, L0_attn_conv_b, L0_attn_W_xproj,
           L0_attn_W_dt, L0_attn_b_dt, L0_attn_A_log, L0_attn_D, L0_attn_W_out,
           L0_attn_norm_w, L0_ffn_w1, L0_ffn_b1, L0_ffn_w2, L0_ffn_b2, L0_c, L0_wmod):
    from concourse.bass_utils import run_bass_kernel_spmd
    x = np.asarray(L0_attn_x)
    c = np.asarray(L0_c)
    wmod = np.ascontiguousarray(L0_wmod)
    cores = list(range(8))

    def own(core):
        return core // 4, slice((core % 4) * T_OWN, (core % 4 + 1) * T_OWN)

    nc1 = _get("L1", build_L1)
    im = []
    for core in cores:
        b, sl = own(core)
        im.append({"x": np.ascontiguousarray(x[b, sl]),
                   "c_sel": np.ascontiguousarray(c[b:b + 1]), "wmod": wmod})
    r1 = run_bass_kernel_spmd(nc1, im, core_ids=cores)
    hs = [np.asarray(r1.results[core]["h"]) for core in cores]
    r2s = [np.ascontiguousarray(np.asarray(r1.results[core]["r2o"]).T).reshape(-1)
           for core in cores]
    h_full = np.concatenate(hs, axis=0)
    r2_full = np.concatenate(r2s).reshape(1, -1)

    nc2 = _get("L2", build_L2)
    W_in = np.asarray(L0_attn_W_in)
    im = []
    for core in cores:
        e0 = core * E_SL
        im.append({
            "h_full": h_full, "r2_full": r2_full,
            "wi_xc": np.ascontiguousarray(W_in[e0:e0 + E_SL]),
            "wi_res": np.ascontiguousarray(W_in[D_INNER + e0:D_INNER + e0 + E_SL]),
            "norm_w": np.ascontiguousarray(np.asarray(L0_attn_norm_w)[None, :]),
            "convw": np.ascontiguousarray(np.asarray(L0_attn_conv_w)[e0:e0 + E_SL]),
            "convb": np.ascontiguousarray(np.asarray(L0_attn_conv_b)[e0:e0 + E_SL]),
            "wxp": np.ascontiguousarray(np.asarray(L0_attn_W_xproj)[:, e0:e0 + E_SL]),
            "wdt": np.ascontiguousarray(np.asarray(L0_attn_W_dt)[e0:e0 + E_SL]),
            "bdt": np.ascontiguousarray(np.asarray(L0_attn_b_dt)[e0:e0 + E_SL]),
            "alog": np.ascontiguousarray(np.asarray(L0_attn_A_log)[e0:e0 + E_SL]),
            "ddvec": np.ascontiguousarray(np.asarray(L0_attn_D)[e0:e0 + E_SL]),
        })
    r2r = run_bass_kernel_spmd(nc2, im, core_ids=cores)
    ygs = [np.asarray(r2r.results[core]["yg"]) for core in cores]

    nc3 = _get("L3", build_L3)
    im = []
    for core in cores:
        b, sl = own(core)
        cols = slice(b * SEQ + sl.start, b * SEQ + sl.stop)
        yg_own = np.concatenate([ygs[cc][:, cols] for cc in cores], axis=0)
        im.append({"yg_own": np.ascontiguousarray(yg_own),
                   "h_own": hs[core],
                   "x_own": np.ascontiguousarray(x[b, sl]),
                   "wout": np.ascontiguousarray(L0_attn_W_out),
                   "c_sel": np.ascontiguousarray(c[b:b + 1]), "wmod": wmod})
    r3 = run_bass_kernel_spmd(nc3, im, core_ids=cores)
    xmids = [np.asarray(r3.results[core]["xmid"]) for core in cores]
    h2s = [np.asarray(r3.results[core]["h2"]) for core in cores]
    gmax = np.asarray(r3.results[0]["gmax"])

    nc4 = _get("L4", build_L4)
    im = [{"h2_own": h2s[core], "gmax": gmax,
           "w1": np.ascontiguousarray(L0_ffn_w1),
           "b1": np.ascontiguousarray(L0_ffn_b1)} for core in cores]
    r4 = run_bass_kernel_spmd(nc4, im, core_ids=cores)
    h1s = [np.asarray(r4.results[core]["h1"]) for core in cores]
    gmax2 = np.asarray(r4.results[0]["gmax2"])

    nc5 = _get("L5", build_L5)
    im = []
    for core in cores:
        b, sl = own(core)
        im.append({"h1_own": h1s[core], "gmax2": gmax2,
                   "w2": np.ascontiguousarray(L0_ffn_w2),
                   "b2": np.ascontiguousarray(L0_ffn_b2),
                   "xmid": xmids[core],
                   "c_sel": np.ascontiguousarray(c[b:b + 1]), "wmod": wmod})
    r5 = run_bass_kernel_spmd(nc5, im, core_ids=cores)

    out = np.empty((NB, SEQ, D_MODEL), dtype=np.float32)
    for core in cores:
        b, sl = own(core)
        out[b, sl, :] = np.asarray(r5.results[core]["outp"]).T
    return out
